# version 03 (final)
# baseline (speedup 1.0000x reference)
import numpy as np
import ml_dtypes
import concourse.bass as bass
import concourse.mybir as mybir
from concourse.bass_utils import run_bass_kernel_spmd

F32 = mybir.dt.float32
BF16 = mybir.dt.bfloat16
AF = mybir.ActivationFunctionType
ALU = mybir.AluOpType
AX = mybir.AxisListType

NCORES = 8
TOK = 2048
D = 1024
KCH = 8
BATCH_T = 8192
NQ = 4

BIAS_COLS = {
    "bq_d": (0, 4), "b_qk": (4, 8), "bkv_d": (12, 1), "bv_u": (13, 8),
    "b_proj": (21, 8), "b_out": (29, 8), "b1": (37, 32), "b2": (69, 8),
}
NBIAS = 80
DEBUG_TAPS = False
L1_INNER = 99
L1_JR = 8
L1_HR = 16

ENGINES = ("sync", "tensor", "vector", "scalar", "gpsimd")
OWN_SEM = {"tensor": "pe", "vector": "ve", "scalar": "ac", "gpsimd": "gp"}


class _Sem:
    __slots__ = ("h", "n")
    def __init__(self, h):
        self.h = h
        self.n = 0


class Sched:
    N_DMA_SEMS = 16

    def __init__(self, nc):
        self.nc = nc
        self.ops = {e: [] for e in ENGINES}
        names = ["pe", "ve", "ac", "gp", "st"]
        names += [f"ld{i}" for i in range(self.N_DMA_SEMS)]
        names += [f"gld{i}" for i in range(self.N_DMA_SEMS)]
        self.sems = {nm: _Sem(nc.alloc_semaphore("sem_" + nm)) for nm in names}
        self.waited = {e: {} for e in ENGINES}
        self._rot = {"sync": 0, "gpsimd": 0}
        self._group = None
        self._pguards = {}
        self._sguards = {}
        self._seq = 0

    def palloc(self, name, shape, dt=F32):
        g = self.nc.psum_tensor(name, shape, dt)
        t = g.__enter__()
        self._seq += 1
        self._pguards[id(t)] = (self._seq, g)
        return t

    def pfree(self, *ts):
        for t in sorted(ts, key=lambda t: -self._pguards[id(t)][0]):
            self._pguards.pop(id(t))[1].__exit__(None, None, None)

    def salloc(self, name, shape, dt, side="left"):
        g = self.nc.sbuf_tensor(name, shape, dt, side=side)
        t = g.__enter__()
        self._seq += 1
        self._sguards[id(t)] = (self._seq, g)
        return t

    def sfree(self, *ts):
        for t in sorted(ts, key=lambda t: -self._sguards[id(t)][0]):
            self._sguards.pop(id(t))[1].__exit__(None, None, None)

    def _wait(self, eng, nm, val):
        if val <= 0:
            return
        w = self.waited[eng]
        if w.get(nm, 0) >= val:
            return
        w[nm] = val
        h = self.sems[nm].h
        self.ops[eng].append(lambda e, h=h, v=val: e.wait_ge(h, v))

    def emit(self, eng, fn, wait=(), inc=None, inc_n=1):
        for nm, val in wait:
            self._wait(eng, nm, val)
        if inc is None:
            self.ops[eng].append(fn)
            return None
        s = self.sems[inc]
        s.n += inc_n
        h = s.h
        def thunk(e, fn=fn, h=h, a=inc_n):
            fn(e).then_inc(h, a)
        self.ops[eng].append(thunk)
        return s.n

    def dma_group(self, eng="sync"):
        pfx = "gld" if eng == "gpsimd" else "ld"
        i = self._rot[eng] % self.N_DMA_SEMS
        self._rot[eng] += 1
        nm = f"{pfx}{i}"
        self._wait(eng, nm, self.sems[nm].n)
        return nm

    def dma(self, out_ap, in_ap, eng="sync", wait=(), group=None):
        nm = group or self.dma_group(eng)
        v = self.emit(eng, lambda e: e.dma_start(out_ap, in_ap),
                      wait=wait, inc=nm, inc_n=16)
        return (nm, v)

    def dma_out(self, out_ap, in_ap, wait=()):
        return self.emit("sync", lambda e: e.dma_start(out_ap, in_ap),
                         wait=wait, inc="st", inc_n=16)

    def n(self, nm):
        return self.sems[nm].n

    def barrier(self):
        snap = {nm: s.n for nm, s in self.sems.items() if s.n > 0}
        for eng in ENGINES:
            own = OWN_SEM.get(eng)
            for nm, v in snap.items():
                if nm != own:
                    self._wait(eng, nm, v)

    def play(self):
        with self.nc.Block() as blk:
            for eng in ENGINES:
                ops = self.ops[eng]
                if not ops:
                    continue
                getattr(blk, eng)(lambda e, ops=ops: [f(e) for f in ops])



def emit_const_setup(s, C):
    nc = s.nc
    C["ones_col_b"] = nc.alloc_sbuf_tensor("ones_col_b", [128, 1], BF16)
    C["ones_row_b"] = nc.alloc_sbuf_tensor("ones_row_b", [1, 128], BF16)
    C["ones_row_f"] = nc.alloc_sbuf_tensor("ones_row_f", [1, 128], F32)
    C["ones_col_f"] = nc.alloc_sbuf_tensor("ones_col_f", [128, 1], F32)
    s.emit("vector", lambda e: e.memset(C["ones_col_b"][:], 1.0), inc="ve")
    s.emit("vector", lambda e: e.memset(C["ones_row_b"][:], 1.0), inc="ve")
    s.emit("vector", lambda e: e.memset(C["ones_row_f"][:], 1.0), inc="ve")
    s.emit("vector", lambda e: e.memset(C["ones_col_f"][:], 1.0), inc="ve")
    C["eps"] = nc.alloc_sbuf_tensor("eps_c", [1, 1], F32)
    s.emit("vector", lambda e: e.memset(C["eps"][:], 1e-5), inc="ve")
    C["const_ve"] = s.n("ve")


def load_weight_chunked(s, name, hbm_ap, kin, nout, dt=BF16, side="left"):
    kt = kin // 128
    t = s.salloc(name, [128, kt * nout], dt, side=side)
    src = hbm_ap.rearrange("(k p) m -> p k m", p=128)
    dst = t[:].rearrange("p (k m) -> p k m", k=kt)
    n = s.dma(dst, src, eng="gpsimd")
    return t, n


def emit_linear_fm(s, ps_pool, lhs_ap_fn, rhs_tiles, out_write,
                   M_tiles, K_tiles, n_chunks=NQ, nsz=512, wait_first=()):
    evac = {}
    for m in range(M_tiles):
        for n in range(n_chunks):
            pb = (m * n_chunks + n) % 2
            ps = ps_pool[pb]
            w = list(wait_first) if (m, n) == (0, 0) else []
            if pb in evac:
                w.append(evac[pb])
            cnt = None
            for k in range(K_tiles):
                cnt = s.emit("tensor", lambda e, ps=ps, la=lhs_ap_fn(k, m),
                             ra=rhs_tiles[k][:, n * nsz:(n + 1) * nsz],
                             st=(k == 0), sp=(k == K_tiles - 1):
                             e.matmul(ps[:], la, ra, start=st, stop=sp),
                             wait=w if k == 0 else (),
                             inc="pe" if k == K_tiles - 1 else None)
            evac[pb] = out_write(m, n, ps, cnt)
    return evac


def emit_ln(s, C, R):
    nc = s.nc
    SQ = [s.salloc(None, [128, TOK], F32) for _ in range(2)]
    rows = s.salloc(None, [1, 5 * TOK], F32)
    m_row = rows[:, 0:TOK]; msq = rows[:, TOK:2 * TOK]; var = rows[:, 2 * TOK:3 * TOK]
    sd = rows[:, 3 * TOK:4 * TOK]; rstd = rows[:, 4 * TOK:5 * TOK]
    psS = s.palloc(None, [1, TOK]); psQ = s.palloc(None, [1, TOK])
    pe_after = {}
    pe_sum = pe_sq = None
    for k in range(KCH):
        w = [("pe", pe_after[k - 2])] if k >= 2 else []
        a = s.emit("scalar", lambda e, k=k: e.activation(SQ[k % 2][:], R[k][:], AF.Square),
                   wait=w, inc="ac")
        for n in range(NQ):
            pe_sum = s.emit("tensor", lambda e, k=k, n=n: e.matmul(
                psS[:, n * 512:(n + 1) * 512], C["ones_col_f"][:, 0:1],
                R[k][:, n * 512:(n + 1) * 512], start=(k == 0), stop=(k == KCH - 1)),
                inc="pe" if n == NQ - 1 else None)
        for n in range(NQ):
            pe_sq = s.emit("tensor", lambda e, k=k, n=n: e.matmul(
                psQ[:, n * 512:(n + 1) * 512], C["ones_col_f"][:, 0:1],
                SQ[k % 2][:, n * 512:(n + 1) * 512], start=(k == 0), stop=(k == KCH - 1)),
                wait=[("ac", a)], inc="pe" if n == NQ - 1 else None)
        pe_after[k] = s.n("pe")
    v1 = s.emit("vector", lambda e: e.tensor_scalar_mul(m_row, psS[0:1, :], 1.0 / D),
                wait=[("pe", pe_sum)], inc="ve")
    a1 = s.emit("scalar", lambda e: e.activation(msq, m_row, AF.Square),
                wait=[("ve", v1)], inc="ac")
    v2 = s.emit("vector", lambda e: e.scalar_tensor_tensor(
        var, psQ[0:1, :], 1.0 / D, msq, ALU.mult, ALU.subtract),
        wait=[("pe", pe_sq), ("ac", a1)], inc="ve")
    a2 = s.emit("scalar", lambda e: e.activation(sd, var, AF.Sqrt, bias=C["eps"][:]),
                wait=[("ve", v2)], inc="ac")
    v3 = s.emit("vector", lambda e: e.reciprocal(rstd, sd), wait=[("ac", a2)], inc="ve")
    s.pfree(psS, psQ)
    psBm = s.palloc(None, [128, TOK]); psBr = s.palloc(None, [128, TOK])
    pm = pr = None
    for n in range(NQ):
        pm = s.emit("tensor", lambda e, n=n: e.matmul(
            psBm[:, n * 512:(n + 1) * 512], C["ones_row_f"][0:1, :],
            m_row[0:1, n * 512:(n + 1) * 512], start=True, stop=True),
            wait=[("ve", v3)], inc="pe" if n == NQ - 1 else None)
        pr = s.emit("tensor", lambda e, n=n: e.matmul(
            psBr[:, n * 512:(n + 1) * 512], C["ones_row_f"][0:1, :],
            rstd[0:1, n * 512:(n + 1) * 512], start=True, stop=True),
            inc="pe" if n == NQ - 1 else None)
    for k in range(KCH):
        s.emit("vector", lambda e, k=k: e.tensor_sub(R[k][:], R[k][:], psBm[:]),
               wait=[("pe", pr)], inc="ve")
        s.emit("vector", lambda e, k=k: e.tensor_mul(R[k][:], R[k][:], psBr[:]),
               inc="ve")
    s.pfree(psBm, psBr)
    s.sfree(SQ[0], SQ[1], rows)



def emit_l0_attn(s, C, R, ins):
    nc = s.nc
    WPR, g_wpr = load_weight_chunked(s, "wpr", ins["wprojT"], 1024, 1024)
    WVU = s.salloc("wvu", [64, 1024], BF16)
    g_wvu = s.dma(WVU[:], ins["wvuT"], eng="gpsimd")
    WKV, g_wkv = load_weight_chunked(s, "wkv", ins["wkvdT"], 1024, 64, side="right")
    WQD, g_wqd = load_weight_chunked(s, "wqd", ins["wqdT"], 1024, 512, side="right")
    WQK, g_wqk = load_weight_chunked(s, "wqk", ins["wqkT"], 512, 1024, side="right")

    BUFA = [s.salloc(f"bufa{k}", [128, TOK], BF16) for k in range(KCH)]
    CQ = [s.salloc(f"cq{m}", [128, TOK], BF16) for m in range(4)]
    CKV2 = s.salloc("ckv2", [128, BATCH_T], BF16)
    YT = [s.salloc(f"yt{k}", [128, TOK], BF16) for k in range(KCH)]

    for k in range(KCH):
        s.emit("vector", lambda e, k=k: e.tensor_copy(BUFA[k][:], R[k][:]), inc="ve")
    xb_done = s.n("ve")

    psA = s.palloc("psA", [128, 512]); psB2 = s.palloc("psB2", [128, 512])
    bq0 = BIAS_COLS["bq_d"][0]

    def cq_write(m, n, ps, pe_cnt):
        c = s.emit("scalar", lambda e, m=m, n=n, ps=ps: e.activation(
            CQ[m][:, n * 512:(n + 1) * 512], ps[:], AF.Identity,
            bias=C["bias"][:, bq0 + m:bq0 + m + 1]),
            wait=[("pe", pe_cnt)], inc="ac")
        return ("ac", c)

    emit_linear_fm(s, [psA, psB2],
                   lambda k, m: WQD[:, k * 512 + m * 128:k * 512 + (m + 1) * 128],
                   BUFA, cq_write, M_tiles=4, K_tiles=KCH,
                   wait_first=[("ve", xb_done), g_wqd])

    QT = BUFA
    bqk0 = BIAS_COLS["b_qk"][0]

    def q_write(m, n, ps, pe_cnt):
        c = s.emit("scalar", lambda e, m=m, n=n, ps=ps: e.activation(
            QT[m][:, n * 512:(n + 1) * 512], ps[:], AF.Identity,
            bias=C["bias"][:, bqk0 + m:bqk0 + m + 1]),
            wait=[("pe", pe_cnt)], inc="ac")
        return ("ac", c)

    emit_linear_fm(s, [psA, psB2],
                   lambda k, m: WQK[:, k * 1024 + m * 128:k * 1024 + (m + 1) * 128],
                   CQ, q_write, M_tiles=KCH, K_tiles=4,
                   wait_first=[("ac", s.n("ac")), g_wqk])
    q_done_ac = s.n("ac")
    q_pe = s.n("pe")
    s.pfree(psA, psB2)
    s.sfree(WQD, WQK)

    XK = [s.salloc(f"xk{i}", [128, TOK], BF16, side="right") for i in range(2)]
    psKV = s.palloc("psKV", [64, TOK])
    bkv = BIAS_COLS["bkv_d"][0]
    kv_ac = 0
    grp_pe = {}
    for tq in range(4):
        for k in range(KCH):
            i = tq * KCH + k
            w = [("pe", grp_pe[i - 2])] if i >= 2 else [("pe", q_pe)]
            g = s.dma(XK[i % 2][:], ins["xbT"][k * 128:(k + 1) * 128,
                                               tq * TOK:(tq + 1) * TOK],
                      eng="gpsimd", wait=w)
            pe_cnt = None
            for n in range(NQ):
                ww = [g]
                if k == 0:
                    ww += [("ac", max(kv_ac, q_done_ac))]
                pe_cnt = s.emit("tensor", lambda e, k=k, n=n, buf=XK[i % 2]: e.matmul(
                    psKV[:, n * 512:(n + 1) * 512], WKV[:, k * 64:(k + 1) * 64],
                    buf[:, n * 512:(n + 1) * 512], start=(k == 0), stop=(k == KCH - 1)),
                    wait=ww + ([g_wkv] if i == 0 else []) if n == 0 else (),
                    inc="pe" if n == NQ - 1 else None)
            grp_pe[i] = s.n("pe")
        s.emit("scalar", lambda e, tq=tq: e.activation(
            CKV2[0:64, tq * TOK:(tq + 1) * TOK], psKV[:], AF.Identity,
            bias=C["bias"][0:64, bkv:bkv + 1]), wait=[("pe", grp_pe[tq * KCH + KCH - 1])],
            inc="ac")
        kv_ac = s.emit("scalar", lambda e, tq=tq: e.activation(
            CKV2[64:128, tq * TOK:(tq + 1) * TOK], psKV[:], AF.Identity,
            bias=C["bias"][64:128, bkv:bkv + 1]), inc="ac")
    ckv_ac = s.n("ac")
    s.pfree(psKV)
    s.sfree(XK[0], XK[1], WKV)

    NRM = s.salloc("nrm", [1, 64], F32)
    NEGB = s.salloc("negb", [128, 16], F32)
    SQB = s.salloc("sqb", [128, TOK], BF16, side="right")
    psN = s.palloc("psN", [1, TOK])
    psNB = s.palloc("psNB", [128, 16])
    maxk2 = NRM[:, 0:1]
    sq_pe = 0
    red_ve = 0
    for tq in range(4):
        a = s.emit("scalar", lambda e, tq=tq: e.activation(
            SQB[0:64, :], CKV2[0:64, tq * TOK:(tq + 1) * TOK], AF.Square),
            wait=[("ac", ckv_ac), ("pe", sq_pe)], inc="ac")
        pe = None
        for n in range(NQ):
            pe = s.emit("tensor", lambda e, n=n: e.matmul(
                psN[:, n * 512:(n + 1) * 512], C["ones_col_b"][0:64, 0:1],
                SQB[0:64, n * 512:(n + 1) * 512], start=True, stop=True),
                wait=[("ac", a), ("ve", red_ve)] if n == 0 else (),
                inc="pe" if n == NQ - 1 else None)
        sq_pe = pe
        if tq == 0:
            red_ve = s.emit("vector", lambda e: e.reduce_max(maxk2, psN[0:1, :], axis=AX.X),
                            wait=[("pe", pe)], inc="ve")
        else:
            red_ve = s.emit("vector", lambda e: e.reduce_max(NRM[:, 40:41], psN[0:1, :], axis=AX.X),
                            wait=[("pe", pe)], inc="ve")
            s.emit("vector", lambda e: e.tensor_max(maxk2, maxk2, NRM[:, 40:41]), inc="ve")
    for c in range(KCH):
        a = s.emit("scalar", lambda e, c=c: e.activation(SQB[:], QT[c][:], AF.Square),
                   wait=[("ac", q_done_ac), ("pe", sq_pe)], inc="ac")
        for half in range(2):
            h = 2 * c + half
            b = 64 * half
            pe = None
            for n in range(NQ):
                pe = s.emit("tensor", lambda e, b=b, n=n: e.matmul(
                    psN[:, n * 512:(n + 1) * 512], C["ones_col_b"][b:b + 64, 0:1],
                    SQB[b:b + 64, n * 512:(n + 1) * 512], start=True, stop=True),
                    wait=[("ac", a), ("ve", red_ve)] if n == 0 else (),
                    inc="pe" if n == NQ - 1 else None)
            sq_pe = pe
            red_ve = s.emit("vector", lambda e, h=h: e.reduce_max(
                NRM[:, 1 + h:2 + h], psN[0:1, :], axis=AX.X),
                wait=[("pe", pe)], inc="ve")
    ve = s.emit("vector", lambda e: e.tensor_scalar(
        NRM[:, 17:33], NRM[:, 1:17], maxk2, None, ALU.mult), inc="ve")
    a = s.emit("scalar", lambda e: e.activation(
        NRM[:, 17:33], NRM[:, 17:33], AF.Sqrt, scale=float((0.125 * 1.02) ** 2)),
        wait=[("ve", ve)], inc="ac")
    ve = s.emit("vector", lambda e: e.tensor_scalar_mul(NRM[:, 17:33], NRM[:, 17:33], -1.0),
                wait=[("ac", a)], inc="ve")
    pe = s.emit("tensor", lambda e: e.matmul(
        psNB[:], C["ones_row_f"][0:1, :], NRM[0:1, 17:33], start=True, stop=True),
        wait=[("ve", ve)], inc="pe")
    negb_ve = s.emit("vector", lambda e: e.tensor_copy(NEGB[:], psNB[:]),
                     wait=[("pe", pe)], inc="ve")
    s.pfree(psN, psNB)
    s.sfree(SQB)

    VHC = s.salloc("vhc", [128, 65 * 64], BF16)
    PT = [s.salloc(f"pt{i}", [128, 1024], BF16) for i in range(2)]
    OH = s.salloc("oh", [65, 1024], F32)
    TY = s.salloc("ty", [64, 1024], F32)
    RZ = s.salloc("rz", [1, 1024], F32)
    psS = [s.palloc(f"psS{i}", [128, 1024]) for i in range(2)]
    psO = s.palloc("psO", [65, 1024])
    PSX = s.palloc("psX", [128, 1024])
    ones_last = None
    for sblk in range(64):
        ones_last = s.emit("vector", lambda e, sblk=sblk: e.memset(
            VHC[:, sblk * 65 + 64:sblk * 65 + 65], 1.0), inc="ve")
    bvu0 = BIAS_COLS["bv_u"][0]

    last_stt_ve = 0
    psO_free_ve = 0
    last_exp_ac = {}
    for h in range(16):
        base = 64 * (h % 2)
        qc = h // 2
        vhc_prev = {}
        for sblk in range(64):
            half = (sblk % 2) * 512
            w = []
            if sblk <= 1:
                w.append(("ve", last_stt_ve))
            if sblk >= 2:
                w.append(("ve", vhc_prev[sblk - 2]))
            if h == 0 and sblk == 0:
                w += [("ac", ckv_ac), g_wvu, ("ve", max(ones_last, negb_ve))]
            pe = s.emit("tensor", lambda e, sblk=sblk, half=half, h=h: e.matmul(
                PSX[:, half:half + 64], CKV2[0:64, sblk * 128:(sblk + 1) * 128],
                WVU[:, h * 64:(h + 1) * 64], start=True, stop=True),
                wait=w, inc="pe")
            vhc_prev[sblk] = s.emit(
                "vector", lambda e, sblk=sblk, half=half: e.tensor_copy(
                    VHC[:, sblk * 65:sblk * 65 + 64], PSX[:, half:half + 64]),
                wait=[("pe", pe)], inc="ve")
        vhc_done = vhc_prev[63]
        for qh in range(2):
            q0 = qh * 1024
            pv_pe = None
            for sblk in range(64):
                pb = sblk % 2
                w = [("ac", last_exp_ac.get(pb, 0))]
                qk_pe = None
                for n2 in range(2):
                    qk_pe = s.emit("tensor",
                                   lambda e, sblk=sblk, n2=n2, pb=pb, base=base,
                                   qc=qc, q0=q0: e.matmul(
                        psS[pb][:, n2 * 512:(n2 + 1) * 512],
                        CKV2[base:base + 64, sblk * 128:(sblk + 1) * 128],
                        QT[qc][base:base + 64, q0 + n2 * 512:q0 + (n2 + 1) * 512],
                        start=True, stop=True),
                        wait=w if n2 == 0 else (), inc="pe" if n2 == 1 else None)
                last_exp_ac[pb] = exp_ac = s.emit(
                    "scalar", lambda e, pb=pb, h=h: e.activation(
                        PT[pb][:], psS[pb][:], AF.Exp,
                        bias=NEGB[:, h:h + 1], scale=0.125),
                    wait=[("pe", qk_pe)], inc="ac")
                for n2 in range(2):
                    w2 = []
                    if n2 == 0:
                        w2 = [("ac", exp_ac), ("ve", vhc_done)]
                        if sblk == 0:
                            w2.append(("ve", psO_free_ve))
                    pv_pe = s.emit("tensor",
                                   lambda e, sblk=sblk, n2=n2, pb=pb: e.matmul(
                        psO[:, n2 * 512:(n2 + 1) * 512],
                        VHC[:, sblk * 65:(sblk + 1) * 65],
                        PT[pb][:, n2 * 512:(n2 + 1) * 512],
                        start=(sblk == 0), stop=(sblk == 63)),
                        wait=w2, inc="pe" if (n2 == 1 and sblk == 63) else None)
            psO_free_ve = ve1 = s.emit("vector", lambda e: e.tensor_copy(OH[:], psO[:]),
                                       wait=[("pe", pv_pe)], inc="ve")
            ve2 = s.emit("vector", lambda e: e.reciprocal(RZ[:], OH[64:65, :]), inc="ve")
            bc_pe = None
            for n2 in range(2):
                bc_pe = s.emit("tensor", lambda e, n2=n2: e.matmul(
                    PSX[0:64, n2 * 512:(n2 + 1) * 512], C["ones_row_f"][0:1, 0:64],
                    RZ[0:1, n2 * 512:(n2 + 1) * 512], start=True, stop=True),
                    wait=[("ve", ve2)], inc="pe" if n2 == 1 else None)
            s.emit("vector", lambda e: e.tensor_mul(TY[:], OH[0:64, :], PSX[0:64, 0:1024]),
                   wait=[("pe", bc_pe)], inc="ve")
            last_stt_ve = s.emit("vector",
                                 lambda e, qc=qc, base=base, q0=q0: e.tensor_scalar_add(
                YT[qc][base:base + 64, q0:q0 + 1024], TY[:],
                C["bias"][base:base + 64, bvu0 + qc:bvu0 + qc + 1]), inc="ve")
    yt_ve = s.n("ve")
    head_ac = s.n("ac")
    s.pfree(psS[0], psS[1], psO, PSX)
    s.sfree(VHC, PT[0], PT[1], OH, TY, RZ)

    psA = s.palloc("psA2", [128, 512]); psB2 = s.palloc("psB22", [128, 512])
    bp0 = BIAS_COLS["b_proj"][0]

    def proj_write(m, n, ps, pe_cnt):
        c = s.emit("vector", lambda e, m=m, n=n, ps=ps: e.scalar_tensor_tensor(
            R[m][:, n * 512:(n + 1) * 512], ps[:],
            C["bias"][:, bp0 + m:bp0 + m + 1], R[m][:, n * 512:(n + 1) * 512],
            ALU.add, ALU.add),
            wait=[("pe", pe_cnt)], inc="ve")
        return ("ve", c)

    if DEBUG_TAPS:
        nc2 = s.nc
        taps = {
            "dbg_ckv": (CKV2, [128, BATCH_T]),
            "dbg_negb": (NEGB, [128, 16]),
            "dbg_yt0": (YT[0], [128, TOK]),
            "dbg_yt7": (YT[7], [128, TOK]),
            "dbg_q0": (QT[0], [128, TOK]),
            "dbg_cq0": (CQ[0], [128, TOK]),
        }
        w = [("ve", s.n("ve")), ("ac", s.n("ac")), ("pe", s.n("pe"))]
        for nm, (t, shp) in taps.items():
            o = nc2.dram_tensor(nm, shp, BF16 if t.dtype == BF16 else F32,
                                kind="ExternalOutput").ap()
            s.dma_out(o, t[:], wait=w)
    emit_linear_fm(s, [psA, psB2],
                   lambda k, m: WPR[:, k * 1024 + m * 128:k * 1024 + (m + 1) * 128],
                   YT, proj_write, M_tiles=KCH, K_tiles=KCH,
                   wait_first=[("ve", yt_ve), ("ac", head_ac), g_wpr])
    s.pfree(psA, psB2)
    s.sfree(*BUFA, *CQ, *YT, CKV2, WVU, NRM, NEGB, WPR)



def emit_moe(s, C, R, ins):
    nc = s.nc
    XB2 = [s.salloc(f"xb2_{k}", [128, TOK], BF16) for k in range(KCH)]
    xb2_ve = 0
    for k in range(KCH):
        xb2_ve = s.emit("vector", lambda e, k=k: e.tensor_copy(XB2[k][:], R[k][:]), inc="ve")

    RT, g_rt = load_weight_chunked(s, "rtw", ins["routerT"], 1024, 8, dt=F32, side="right")
    onesq = s.salloc("onesq", [128, 128], F32)
    ident = s.salloc("identf", [128, 128], F32)
    LG = s.salloc("lg", [128, 8], F32, side="right")
    NMX = s.salloc("nmx", [128, 1], F32, side="right")
    SCc = s.salloc("sccol", [128, 4], F32, side="right")
    SG8 = s.salloc("sg8", [128, 8], F32, side="right")
    SME = s.salloc("sme", [128, 128], F32)
    SCt = s.salloc("sct", [16, 128], F32)
    MROW = s.salloc("mrow", [1, TOK], F32)

    ve0 = s.emit("vector", lambda e: e.memset(onesq[:], 1.0), inc="ve")
    gp0 = s.emit("gpsimd", lambda e: e.affine_select(
        ident[:], onesq[:], pattern=[[-1, 128]], compare_op=ALU.is_equal,
        fill=0.0, base=0, channel_multiplier=1), wait=[("ve", ve0)], inc="gp")

    psR = s.palloc("psR", [128, 8])
    lg_ve = 0
    sc_ac = 0
    sg_ac = 0
    for tt in range(16):
        pe = None
        for k in range(KCH):
            pe = s.emit("tensor", lambda e, k=k, tt=tt: e.matmul(
                psR[:], R[k][:, tt * 128:(tt + 1) * 128], RT[:, k * 8:(k + 1) * 8],
                start=(k == 0), stop=(k == KCH - 1)),
                wait=[("ve", lg_ve), g_rt] if k == 0 else (),
                inc="pe" if k == KCH - 1 else None)
        lg_ve = s.emit("vector", lambda e: e.tensor_copy(LG[:], psR[:]),
                       wait=[("pe", pe), ("ac", sg_ac)], inc="ve")
        nmx_ve = s.emit("vector", lambda e: e.reduce_max(
            NMX[:], psR[:], axis=AX.X, negate=True), inc="ve")
        sc_ac = s.emit("scalar", lambda e: e.activation(
            SCc[:, 0:1], NMX[:, 0:1], AF.Sigmoid, scale=-1.0),
            wait=[("ve", nmx_ve)], inc="ac")
        sg_ac = s.emit("scalar", lambda e: e.activation(
            SG8[:], LG[:], AF.Sign, bias=NMX[:, 0:1]), inc="ac")
        oh_ve = s.emit("vector", lambda e: e.tensor_scalar_add(SG8[:], SG8[:], 1.0),
                       wait=[("ac", sg_ac)], inc="ve")
        lg_ve = s.emit("vector", lambda e, tt=tt: e.tensor_scalar(
            SME[:].rearrange("p (g t) -> p g t", t=16)[:, :, tt], SG8[:],
            SCc[:, 0:1], None, ALU.mult), inc="ve")
    s.pfree(psR)
    s.sfree(RT, LG, NMX, SCc, SG8)
    sme_ve = s.n("ve")
    router_done = [("ve", s.n("ve")), ("ac", s.n("ac")), ("pe", s.n("pe")),
                   ("gp", s.n("gp"))]

    XSE = [s.salloc(f"xse{k}", [128, 1024], BF16) for k in range(KCH)]
    GT = s.salloc("gtmp", [128, 1024], BF16)
    HB = [s.salloc(f"hb{f}", [128, 1024], BF16) for f in range(8)]
    GW = [s.salloc(f"gw{k}", [128, 1024], BF16, side="right") for k in range(KCH)]
    UW = [s.salloc(f"uw{k}", [128, 1024], BF16, side="right") for k in range(KCH)]
    DW = [s.salloc(f"dw{k}", [128, 1024], BF16, side="right") for k in range(KCH)]
    psXB = s.palloc("psXB", [128, 1024])
    psG = [s.palloc(f"psG{i}", [128, 1024]) for i in range(2)]
    psMM = s.palloc("psMM", [128, 1024])

    mrow_ve = 0
    prev_gmr = ("pe", 0)
    xse_ve = 0
    xse_pe = 0
    guw_pe = 0
    dw_pe = 0
    hb_pe = 0
    acc_ve = 0
    silu_ac = 0
    stt_ve = {}
    for e9 in range(9):
        shared = e9 == 8
        if not shared:
            psT = psXB[0:16, 0:128]
            pe_t = s.emit("tensor", lambda e, e9=e9: e.transpose(
                psT, SME[:, e9 * 16:(e9 + 1) * 16], ident[:]),
                wait=[("ve", max(sme_ve, xse_ve)), ("gp", gp0), ("pe", xse_pe)], inc="pe")
            ve_t = s.emit("vector", lambda e: e.tensor_copy(SCt[:], psT),
                          wait=[("pe", pe_t), prev_gmr], inc="ve")
            gmr = s.dma_group("sync")
            g_mr2 = None
            for r in range(16):
                g_mr2 = s.dma(MROW[0:1, r * 128:(r + 1) * 128], SCt[r:r + 1, :],
                              wait=[("ve", ve_t), ("pe", xse_pe)] if r == 0 else (),
                              group=gmr)
            mrow_ve = 0
            mrow_g = g_mr2
            prev_gmr = g_mr2
        for th in range(2):
            t0 = th * 1024
            if shared:
                xse_ve = 0
                for k in range(KCH):
                    xse_ve = s.emit("vector", lambda e, k=k, t0=t0: e.tensor_copy(
                        XSE[k][:], XB2[k][:, t0:t0 + 1024]),
                        wait=[("pe", xse_pe), ("ve", xb2_ve)] if k == 0 else (), inc="ve")
            else:
                bc_pe = None
                for n2 in range(2):
                    bc_pe = s.emit("tensor", lambda e, n2=n2, t0=t0: e.matmul(
                        psXB[:, n2 * 512:(n2 + 1) * 512], C["ones_row_f"][0:1, :],
                        MROW[0:1, t0 + n2 * 512:t0 + (n2 + 1) * 512],
                        start=True, stop=True),
                        wait=[mrow_g, ("ve", xse_ve if (e9 or th) else 0)]
                        if n2 == 0 else (), inc="pe" if n2 == 1 else None)
                xse_ve = 0
                for k in range(KCH):
                    xse_ve = s.emit("vector", lambda e, k=k, t0=t0: e.tensor_mul(
                        XSE[k][:], XB2[k][:, t0:t0 + 1024], psXB[:]),
                        wait=[("pe", max(bc_pe, xse_pe)), ("ve", xb2_ve)]
                        if k == 0 else (), inc="ve")
            for ffh in range(2):
                f0 = ffh * 1024
                if shared:
                    gsrc = ins["shared_gate"]; usrc = ins["shared_up"]; dsrc = ins["shared_down"]
                else:
                    gsrc = ins["gate"][e9]; usrc = ins["up"][e9]; dsrc = ins["down"][e9]
                gg = s.dma_group("gpsimd"); gu = s.dma_group("gpsimd")
                g_g = g_u = None
                guw = [("pe", guw_pe)] + (router_done if guw_pe == 0 else [])
                for k in range(KCH):
                    g_g = s.dma(GW[k][:], gsrc[k * 128:(k + 1) * 128, f0:f0 + 1024],
                                eng="gpsimd", group=gg, wait=guw if k == 0 else ())
                    g_u = s.dma(UW[k][:], usrc[k * 128:(k + 1) * 128, f0:f0 + 1024],
                                eng="gpsimd", group=gu)
                for fft in range(8):
                    pb = fft % 2
                    g_pe = None
                    for n2 in range(2):
                        w = []
                        if n2 == 0:
                            w = [("ve", xse_ve), g_g, ("ve", stt_ve.get(pb, 0))]
                        for k in range(KCH):
                            g_pe = s.emit("tensor",
                                          lambda e, k=k, n2=n2, fft=fft, pb=pb: e.matmul(
                                psG[pb][:, n2 * 512:(n2 + 1) * 512],
                                GW[k][:, fft * 128:(fft + 1) * 128],
                                XSE[k][:, n2 * 512:(n2 + 1) * 512],
                                start=(k == 0), stop=(k == KCH - 1)),
                                wait=w if k == 0 else (),
                                inc="pe" if (k == KCH - 1 and n2 == 1) else None)
                    silu_ac = s.emit("scalar", lambda e, pb=pb: e.activation(
                        GT[:], psG[pb][:], AF.Silu),
                        wait=[("pe", g_pe), ("ve", stt_ve.get(1 - pb, 0))], inc="ac")
                    u_pe = None
                    for n2 in range(2):
                        w = [("ac", silu_ac), g_u] if n2 == 0 else ()
                        for k in range(KCH):
                            u_pe = s.emit("tensor",
                                          lambda e, k=k, n2=n2, fft=fft, pb=pb: e.matmul(
                                psG[pb][:, n2 * 512:(n2 + 1) * 512],
                                UW[k][:, fft * 128:(fft + 1) * 128],
                                XSE[k][:, n2 * 512:(n2 + 1) * 512],
                                start=(k == 0), stop=(k == KCH - 1)),
                                wait=w if k == 0 else (),
                                inc="pe" if (k == KCH - 1 and n2 == 1) else None)
                    stt_ve[pb] = s.emit("vector", lambda e, fft=fft, pb=pb: e.scalar_tensor_tensor(
                        HB[fft][:], psG[pb][:], 1.0, GT[:], ALU.bypass, ALU.mult),
                        wait=[("pe", u_pe), ("pe", hb_pe) if fft == 0 else ("pe", 0)],
                        inc="ve")
                hb_ve = s.n("ve")
                guw_pe = s.n("pe")
                gd = s.dma_group("gpsimd")
                g_d = None
                for k in range(KCH):
                    g_d = s.dma(DW[k][:], dsrc[f0 + k * 128:f0 + (k + 1) * 128, :],
                                eng="gpsimd", group=gd,
                                wait=[("pe", dw_pe)] if k == 0 else ())
                for m in range(KCH):
                    d_pe = None
                    for n2 in range(2):
                        w = []
                        if n2 == 0:
                            w = [("ve", hb_ve), g_d, ("ve", acc_ve)]
                        for k in range(KCH):
                            d_pe = s.emit("tensor",
                                          lambda e, k=k, m=m, n2=n2: e.matmul(
                                psMM[:, n2 * 512:(n2 + 1) * 512],
                                DW[k][:, m * 128:(m + 1) * 128],
                                HB[k][:, n2 * 512:(n2 + 1) * 512],
                                start=(k == 0), stop=(k == KCH - 1)),
                                wait=w if k == 0 else (),
                                inc="pe" if (k == KCH - 1 and n2 == 1) else None)
                    acc_ve = s.emit("vector", lambda e, m=m, t0=t0,
                                    d_pe=d_pe: e.scalar_tensor_tensor(
                        R[m][:, t0:t0 + 1024], psMM[:], 1.0,
                        R[m][:, t0:t0 + 1024], ALU.bypass, ALU.add),
                        wait=[("pe", d_pe)], inc="ve")
                hb_pe = s.n("pe")
                dw_pe = s.n("pe")
            xse_pe = s.n("pe")
    s.barrier()
    if DEBUG_TAPS:
        for nm, t, shp in (("dbg_sme", SME, [128, 128]), ("dbg_sct", SCt, [16, 128]),
                           ("dbg_mrow", MROW, [1, TOK]), ("dbg_xse0", XSE[0], [128, 1024])):
            oo = s.nc.dram_tensor(nm, shp, t.dtype, kind="ExternalOutput").ap()
            s.dma_out(oo, t[:])
        s.barrier()
    s.pfree(psXB, psG[0], psG[1], psMM)
    s.sfree(*XB2, MROW, *XSE, GT, *HB, *GW, *UW, *DW, onesq, ident, SME, SCt)


def emit_pack(s, R, dst_ap):
    PK = s.salloc("pk", [128, 160], F32, side="right")
    for k in range(KCH):
        v1 = s.emit("vector", lambda e, k=k: e.tensor_copy(PK[:, 0:64], R[k][:, 0:64]), inc="ve")
        v2 = s.emit("vector", lambda e, k=k: e.tensor_copy(PK[:, 64:128], R[k][:, TOK - 64:TOK]), inc="ve")
        v3 = s.emit("vector", lambda e, k=k: e.tensor_copy(
            PK[:, 128:160], R[k][:].rearrange("p (g j) -> p g j", j=64)[:, :, 0]), inc="ve")
        g = s.dma(dst_ap[k * 128:(k + 1) * 128, :], PK[:], wait=[("ve", v3)])
        s.emit("vector", lambda e: e.nop(), wait=[g], inc="ve")
    s.sfree(PK)



def emit_l1_attn(s, C, R, ins, stages=99):
    nc = s.nc
    H = 16
    NEGBG = s.salloc("l1nbg", [128, 16], F32)
    NEGBL = s.salloc("l1nbl", [128, 16], F32)
    NRM = s.salloc("l1nrm", [1, 128], F32)
    KGT = s.salloc("l1kg", [128, 8 * 128], BF16)
    VGH = s.salloc("l1vg", [128, 1040], BF16)
    MASKL = s.salloc("l1ml", [128, 1], F32)
    MASKR = s.salloc("l1mr", [64, 1], F32)
    Y1 = [s.salloc(f"l1y{k}", [128, 1024], BF16) for k in range(KCH)]
    QT = [s.salloc(f"l1q{k}", [128, 1024], BF16) for k in range(KCH)]
    KT = [s.salloc(f"l1k{k}", [128, 1152], BF16) for k in range(KCH)]
    VS = [s.salloc(f"l1vs{j}", [128, 1040], BF16) for j in range(9)]
    g_ml = s.dma(MASKL[:], ins["maskL"])
    g_mr = s.dma(MASKR[:], ins["maskR"])
    XHL, g_hl = load_weight_chunked(s, "l1xhl", ins["haloL"], 1024, 64, side="right")
    XHR, g_hr = load_weight_chunked(s, "l1xhr", ins["haloR"], 1024, 64, side="right")
    XHG, g_hg = load_weight_chunked(s, "l1xhg", ins["haloG"], 1024, 128, side="right")
    XBE = [s.salloc(f"l1xbe{k}", [128, 1152], BF16, side="right") for k in range(KCH)]

    WCUR = s.salloc("l1wcur", [128, 8192], BF16, side="right")

    def load_w(col0, col1, wait_pe):
        src_ap = (ins["wqkv"][:, col0:col1] if col1 <= 3072 else ins["wout"]
                  ).rearrange("(k p) m -> p k m", p=128)
        dst = WCUR[:].rearrange("p (k m) -> p k m", k=KCH)
        return s.dma(dst, src_ap, eng="gpsimd", wait=[("pe", wait_pe)])

    g_wk = load_w(1024, 2048, 0)
    psA = s.palloc("l1psA", [128, 512])
    psB = s.palloc("l1psB", [128, 512])
    psW = s.palloc("l1psW", [128, 1024])
    kg_ve = 0
    for m in range(KCH):
        pe = None
        for k in range(KCH):
            pe = s.emit("tensor", lambda e, k=k, m=m: e.matmul(
                psA[:, 0:128], WCUR[:, k * 1024 + m * 128:k * 1024 + (m + 1) * 128],
                XHG[:, k * 128:(k + 1) * 128], start=(k == 0), stop=(k == KCH - 1)),
                wait=[g_hg, g_wk, ("ve", kg_ve)] if k == 0 else (),
                inc="pe" if k == KCH - 1 else None)
        kg_ve = s.emit("vector", lambda e, m=m: e.tensor_copy(
            KGT[:, m * 128:(m + 1) * 128], psA[:, 0:128]), wait=[("pe", pe)], inc="ve")
    g_wv = load_w(2048, 3072, s.n("pe"))
    z_ve = s.emit("vector", lambda e: e.memset(
        VGH[:].rearrange("p (h c) -> p h c", c=65)[:, :, 64], 1.0), inc="ve")
    pe = None
    for n2 in range(2):
        for k in range(KCH):
            pe = s.emit("tensor", lambda e, k=k, n2=n2: e.matmul(
                psW[:, n2 * 512:(n2 + 1) * 512], XHG[:, k * 128:(k + 1) * 128],
                WCUR[:, k * 1024 + n2 * 512:k * 1024 + (n2 + 1) * 512],
                start=(k == 0), stop=(k == KCH - 1)),
                wait=[g_wv] if (k == 0 and n2 == 0) else (),
                inc="pe" if (k == KCH - 1 and n2 == 1) else None)
    vg_ve = s.emit("vector", lambda e: e.tensor_copy(
        VGH[:].rearrange("p (h c) -> p h c", c=65)[:, :, 0:64],
        psW[:].rearrange("p (h c) -> p h c", c=64)),
        wait=[("pe", pe), ("ve", z_ve)], inc="ve")

    if stages <= 1:
        return
    SQB = s.salloc("l1sqb", [128, 1152], BF16, side="right")
    psN = s.palloc("l1psN0", [1, 1152])
    psNB = s.palloc("l1psNB0", [128, 16])
    red_ve = 0
    sq_pe = s.n("pe")
    for c in range(KCH):
        a = s.emit("scalar", lambda e, c=c: e.activation(
            SQB[:, 0:128], KGT[:, c * 128:(c + 1) * 128], AF.Square),
            wait=[("pe", sq_pe), ("ve", kg_ve)], inc="ac")
        for half in range(2):
            h = 2 * c + half
            b = 64 * half
            pe = s.emit("tensor", lambda e, b=b: e.matmul(
                psN[:, 0:128], C["ones_col_b"][b:b + 64, 0:1],
                SQB[b:b + 64, 0:128], start=True, stop=True),
                wait=[("ac", a), ("ve", red_ve)], inc="pe")
            sq_pe = pe
            red_ve = s.emit("vector", lambda e, h=h: e.reduce_max(
                NRM[:, 32 + h:33 + h], psN[0:1, 0:128], axis=AX.X),
                wait=[("pe", pe)], inc="pe" if False else "ve")

    s.pfree(psN, psNB, psA, psB, psW)
    PL1 = s.salloc("l1pl1", [128, 128], BF16)
    PL2 = s.salloc("l1pl2", [64, 128], BF16)
    PG = s.salloc("l1pg", [128, 128], BF16)
    OL = s.salloc("l1ol", [65, 128], F32)
    OG = s.salloc("l1og", [65, 128], F32)
    RZ2 = s.salloc("l1rz", [1, 256], F32)
    T1 = s.salloc("l1t1", [64, 128], F32)
    T2 = s.salloc("l1t2", [64, 128], F32)

    last = {"expL1": 0, "expL2": 0, "expG": 0, "gpL1": 0, "gpL2": 0,
            "pvl": 0, "pvg": 0, "ol": 0, "og": 0, "y": 0, "ml": 0}
    proj_ve = 0
    y_war = 0
    for th in range(2):
        t0 = th * 1024
        xbe_ve = 0
        pe_prev = s.n("pe")
        for k in range(KCH):
            w = [("pe", pe_prev)] if k == 0 else []
            if th == 0:
                s.emit("vector", lambda e, k=k: e.tensor_copy(
                    XBE[k][:, 0:64], XHL[:, k * 64:(k + 1) * 64]),
                    wait=w + [g_hl], inc="ve")
                s.emit("vector", lambda e, k=k: e.tensor_copy(
                    XBE[k][:, 1088:1152], R[k][:, 1024:1088]), inc="ve")
            else:
                s.emit("vector", lambda e, k=k: e.tensor_copy(
                    XBE[k][:, 0:64], XBE[k][:, 1024:1088]), wait=w, inc="ve")
                s.emit("vector", lambda e, k=k: e.tensor_copy(
                    XBE[k][:, 1088:1152], XHR[:, k * 64:(k + 1) * 64]),
                    wait=[g_hr], inc="ve")
            xbe_ve = s.emit("vector", lambda e, k=k, t0=t0: e.tensor_copy(
                XBE[k][:, 64:1088], R[k][:, t0:t0 + 1024]), inc="ve")

        psA = s.palloc(f"l1psA_{th}", [128, 512])
        psB = s.palloc(f"l1psB_{th}", [128, 512])
        psW = s.palloc(f"l1psW_{th}", [128, 1024])
        psN = s.palloc(f"l1psN_{th}", [1, 1152])
        psNB = s.palloc(f"l1psNB_{th}", [128, 16])
        g_wq = load_w(0, 1024, s.n("pe"))

        def q_write(m, n, ps, pe_cnt):
            c2 = s.emit("vector", lambda e, m=m, n=n, ps=ps: e.tensor_copy(
                QT[m][:, n * 512:(n + 1) * 512], ps[:]), wait=[("pe", pe_cnt)], inc="ve")
            return ("ve", c2)

        emit_linear_fm(s, [psA, psB],
                       lambda k, m: WCUR[:, k * 1024 + m * 128:k * 1024 + (m + 1) * 128],
                       [x[:, 64:1088] for x in XBE], q_write, M_tiles=KCH, K_tiles=KCH,
                       n_chunks=2, wait_first=[("ve", xbe_ve), g_wq, ("ac", s.n("ac"))])
        g_wk2 = load_w(1024, 2048, s.n("pe"))
        kn_chunks = [(0, 512), (512, 512), (1024, 128)]
        kt_ve = 0
        evk = {}
        for m in range(KCH):
            for (c0, w2) in kn_chunks:
                pb = (c0 // 512) % 2
                ps = [psA, psB][pb]
                pe = None
                w_first = [("ve", s.n("ve")), g_wk2] if (m == 0 and c0 == 0) else []
                if pb in evk:
                    w_first.append(("ve", evk[pb]))
                for k in range(KCH):
                    pe = s.emit("tensor", lambda e, k=k, m=m, c0=c0, w2=w2, ps=ps: e.matmul(
                        ps[:, 0:w2], WCUR[:, k * 1024 + m * 128:k * 1024 + (m + 1) * 128],
                        XBE[k][:, c0:c0 + w2], start=(k == 0), stop=(k == KCH - 1)),
                        wait=w_first if k == 0 else (),
                        inc="pe" if k == KCH - 1 else None)
                evk[pb] = kt_ve = s.emit("vector",
                                         lambda e, m=m, c0=c0, w2=w2, ps=ps: e.tensor_copy(
                    KT[m][:, c0:c0 + w2], ps[:, 0:w2]), wait=[("pe", pe)], inc="ve")
        g_wv2 = load_w(2048, 3072, s.n("pe"))
        vs_ve = 0
        for j in range(9):
            c0 = j * 128 + 32
            nr = 64 if j == 8 else 128
            pe = None
            for n2 in range(2):
                for k in range(KCH):
                    pe = s.emit("tensor", lambda e, k=k, n2=n2, c0=c0, nr=nr: e.matmul(
                        psW[0:nr, n2 * 512:(n2 + 1) * 512], XBE[k][:, c0:c0 + nr],
                        WCUR[:, k * 1024 + n2 * 512:k * 1024 + (n2 + 1) * 512],
                        start=(k == 0), stop=(k == KCH - 1)),
                        wait=[("ve", vs_ve), g_wv2] if (k == 0 and n2 == 0) else (),
                        inc="pe" if (k == KCH - 1 and n2 == 1) else None)
            if th == 0:
                vs_ve = s.emit("vector", lambda e, j=j: e.memset(
                    VS[j][:].rearrange("p (h c) -> p h c", c=65)[:, :, 64], 1.0), inc="ve")
            vs_ve = s.emit("vector", lambda e, j=j, nr=nr: e.tensor_copy(
                VS[j][0:nr, :].rearrange("p (h c) -> p h c", c=65)[:, :, 0:64],
                psW[0:nr, :].rearrange("p (h c) -> p h c", c=64)),
                wait=[("pe", pe)], inc="ve")
        prj_pe = s.n("pe")

        if stages <= 2:
            return
        red_ve2 = 0
        sq_pe = prj_pe
        for c in range(KCH):
            a = s.emit("scalar", lambda e, c=c: e.activation(
                SQB[:, 0:1024], QT[c][:], AF.Square),
                wait=[("pe", sq_pe), ("ve", s.n("ve"))], inc="ac")
            for half in range(2):
                h = 2 * c + half
                b = 64 * half
                pe = None
                for n2 in range(2):
                    pe = s.emit("tensor", lambda e, b=b, n2=n2: e.matmul(
                        psN[:, n2 * 512:(n2 + 1) * 512], C["ones_col_b"][b:b + 64, 0:1],
                        SQB[b:b + 64, n2 * 512:(n2 + 1) * 512], start=True, stop=True),
                        wait=[("ac", a), ("ve", red_ve2)] if n2 == 0 else (),
                        inc="pe" if n2 == 1 else None)
                sq_pe = pe
                red_ve2 = s.emit("vector", lambda e, h=h: e.reduce_max(
                    NRM[:, h:h + 1], psN[0:1, 0:1024], axis=AX.X),
                    wait=[("pe", pe)], inc="ve")
        for c in range(KCH):
            a = s.emit("scalar", lambda e, c=c: e.activation(
                SQB[:, 0:1152], KT[c][:], AF.Square),
                wait=[("pe", sq_pe), ("ve", kt_ve)], inc="ac")
            for half in range(2):
                h = 2 * c + half
                b = 64 * half
                pe = None
                for (c0, w2) in kn_chunks:
                    pe = s.emit("tensor", lambda e, b=b, c0=c0, w2=w2: e.matmul(
                        psN[:, c0:c0 + w2], C["ones_col_b"][b:b + 64, 0:1],
                        SQB[b:b + 64, c0:c0 + w2], start=True, stop=True),
                        wait=[("ac", a), ("ve", red_ve2)] if c0 == 0 else (),
                        inc="pe" if c0 == 1024 else None)
                sq_pe = pe
                red_ve2 = s.emit("vector", lambda e, h=h: e.reduce_max(
                    NRM[:, 16 + h:17 + h], psN[0:1, 0:1152], axis=AX.X),
                    wait=[("pe", pe)], inc="ve")
        ve = s.emit("vector", lambda e: e.tensor_mul(NRM[:, 48:64], NRM[:, 0:16], NRM[:, 16:32]),
                    wait=[("ve", red_ve2)], inc="ve")
        a = s.emit("scalar", lambda e: e.activation(
            NRM[:, 48:64], NRM[:, 48:64], AF.Sqrt, scale=float((0.125 * 1.02) ** 2)),
            wait=[("ve", ve)], inc="ac")
        ve = s.emit("vector", lambda e: e.tensor_scalar_mul(NRM[:, 48:64], NRM[:, 48:64], -1.0),
                    wait=[("ac", a)], inc="ve")
        pe = s.emit("tensor", lambda e: e.matmul(
            psNB[:], C["ones_row_f"][0:1, :], NRM[0:1, 48:64], start=True, stop=True),
            wait=[("ve", ve), ("ve", last["y"])], inc="pe")
        nbl_ve = s.emit("vector", lambda e: e.tensor_copy(NEGBL[:], psNB[:]),
                        wait=[("pe", pe), ("ac", s.n("ac"))], inc="ve")
        ve2 = s.emit("vector", lambda e: e.tensor_mul(NRM[:, 64:80], NRM[:, 0:16], NRM[:, 32:48]),
                     inc="ve")
        a2 = s.emit("scalar", lambda e: e.activation(
            NRM[:, 64:80], NRM[:, 64:80], AF.Sqrt, scale=float((0.125 * 1.02) ** 2)),
            wait=[("ve", ve2)], inc="ac")
        ve2 = s.emit("vector", lambda e: e.tensor_scalar_mul(NRM[:, 64:80], NRM[:, 64:80], -1.0),
                     wait=[("ac", a2)], inc="ve")
        pe = s.emit("tensor", lambda e: e.matmul(
            psNB[:], C["ones_row_f"][0:1, :], NRM[0:1, 64:80], start=True, stop=True),
            wait=[("ve", ve2)], inc="pe")
        nbg_ve = s.emit("vector", lambda e: e.tensor_copy(NEGBG[:], psNB[:]),
                        wait=[("pe", pe)], inc="ve")

        if stages <= 3:
            return
        s.pfree(psN, psNB, psA, psB, psW)
        psL1 = s.palloc(f"psL1_{th}", [128, 128])
        psL2 = s.palloc(f"psL2_{th}", [64, 128])
        psG2 = s.palloc(f"psG2_{th}", [128, 128])
        psLO = s.palloc(f"psLO_{th}", [65, 128])
        psGO = s.palloc(f"psGO_{th}", [65, 128])
        psBL = s.palloc(f"psBL_{th}", [64, 128])
        psBG = s.palloc(f"psBG_{th}", [64, 128])
        first_w = [("ve", max(nbl_ve, nbg_ve, vg_ve, vs_ve, s.n("ve"))),
                   ("ac", s.n("ac")), g_ml, g_mr]
        for jl in range(L1_JR):
            for h in range(min(H, L1_HR)):
                base = 64 * (h % 2)
                qc = h // 2
                col0 = jl * 128 + 32
                w0 = list(first_w) if (jl == 0 and h == 0) else []
                p1 = s.emit("tensor", lambda e, qc=qc, base=base, col0=col0, jl=jl: e.matmul(
                    psL1[:], KT[qc][base:base + 64, col0:col0 + 128],
                    QT[qc][base:base + 64, jl * 128:(jl + 1) * 128], start=True, stop=True),
                    wait=w0 + [("ac", last["expL1"])], inc="pe")
                p2 = s.emit("tensor", lambda e, qc=qc, base=base, col0=col0, jl=jl: e.matmul(
                    psL2[:], KT[qc][base:base + 64, col0 + 128:col0 + 192],
                    QT[qc][base:base + 64, jl * 128:(jl + 1) * 128], start=True, stop=True),
                    wait=[("ac", last["expL2"])], inc="pe")
                last["expL1"] = e1 = s.emit("scalar", lambda e, h=h: e.activation(
                    PL1[:], psL1[:], AF.Exp, bias=NEGBL[:, h:h + 1], scale=0.125),
                    wait=[("pe", max(p1, last["pvl"]))], inc="ac")
                last["expL2"] = e2 = s.emit("scalar", lambda e, h=h: e.activation(
                    PL2[:], psL2[:], AF.Exp, bias=NEGBL[0:64, h:h + 1], scale=0.125),
                    wait=[("pe", max(p2, last["pvl"]))], inc="ac")
                if L1_INNER <= 1:
                    last["pvl"] = s.n("pe"); continue
                s.emit("gpsimd", lambda e: e.affine_select(
                    PL1[:], PL1[:], pattern=[[-1, 128]], compare_op=ALU.is_ge,
                    fill=0.0, base=0, channel_multiplier=1),
                    wait=[("ac", e1)], inc="gp")
                g1 = s.emit("gpsimd", lambda e: e.affine_select(
                    PL1[:], PL1[:], pattern=[[1, 128]], compare_op=ALU.is_ge,
                    fill=0.0, base=63, channel_multiplier=-1), inc="gp")
                g2 = s.emit("gpsimd", lambda e: e.affine_select(
                    PL2[:], PL2[:], pattern=[[1, 128]], compare_op=ALU.is_ge,
                    fill=0.0, base=-65, channel_multiplier=-1),
                    wait=[("ac", e2)], inc="gp")
                if th == 0 and jl == 0:
                    last["ml"] = s.emit("vector", lambda e: e.tensor_scalar(
                        PL1[:], PL1[:], MASKL[:], None, ALU.mult),
                        wait=[("gp", g1)], inc="ve")
                if th == 1 and jl == 7:
                    last["ml"] = s.emit("vector", lambda e: e.tensor_scalar(
                        PL2[:], PL2[:], MASKR[:], None, ALU.mult),
                        wait=[("gp", g2)], inc="ve")
                if L1_INNER <= 2:
                    last["pvl"] = s.n("pe"); continue
                wpv = [("gp", g1), ("gp", g2), ("ve", last["ol"]), ("ve", last["ml"])]
                s.emit("tensor", lambda e, jl=jl, h=h: e.matmul(
                    psLO[:], VS[jl][:].rearrange("p (g c) -> p g c", c=65)[:, h],
                    PL1[:], start=True, stop=False), wait=wpv, inc="pe")
                last["pvl"] = pvl = s.emit("tensor", lambda e, jl=jl, h=h: e.matmul(
                    psLO[:], VS[jl + 1][0:64, :].rearrange("p (g c) -> p g c", c=65)[:, h],
                    PL2[:], start=False, stop=True), inc="pe")
                if L1_INNER <= 3:
                    continue
                pg = s.emit("tensor", lambda e, qc=qc, base=base, jl=jl: e.matmul(
                    psG2[:], KGT[base:base + 64, qc * 128:(qc + 1) * 128],
                    QT[qc][base:base + 64, jl * 128:(jl + 1) * 128], start=True, stop=True),
                    wait=[("ac", last["expG"])], inc="pe")
                last["expG"] = eg = s.emit("scalar", lambda e, h=h: e.activation(
                    PG[:], psG2[:], AF.Exp, bias=NEGBG[:, h:h + 1], scale=0.125),
                    wait=[("pe", max(pg, last["pvg"]))], inc="ac")
                last["pvg"] = pvg = s.emit("tensor", lambda e, h=h: e.matmul(
                    psGO[:], VGH[:].rearrange("p (g c) -> p g c", c=65)[:, h],
                    PG[:], start=True, stop=True),
                    wait=[("ac", eg), ("ve", last["og"])], inc="pe")
                if L1_INNER <= 4:
                    continue
                last["ol"] = ol = s.emit("vector", lambda e: e.tensor_copy(OL[:], psLO[:]),
                                         wait=[("pe", pvl)], inc="ve")
                rl = s.emit("vector", lambda e: e.reciprocal(RZ2[:, 0:128], OL[64:65, :]),
                            inc="ve")
                last["og"] = og = s.emit("vector", lambda e: e.tensor_copy(OG[:], psGO[:]),
                                         wait=[("pe", pvg)], inc="ve")
                rg = s.emit("vector", lambda e: e.reciprocal(RZ2[:, 128:256], OG[64:65, :]),
                            inc="ve")
                bl = s.emit("tensor", lambda e: e.matmul(
                    psBL[:], C["ones_row_f"][0:1, 0:64], RZ2[0:1, 0:128],
                    start=True, stop=True), wait=[("ve", rl), ("ve", last["y"])], inc="pe")
                bg = s.emit("tensor", lambda e: e.matmul(
                    psBG[:], C["ones_row_f"][0:1, 0:64], RZ2[0:1, 128:256],
                    start=True, stop=True), wait=[("ve", rg)], inc="pe")
                v1 = s.emit("vector", lambda e: e.tensor_mul(T1[:], OL[0:64, :], psBL[:]),
                            wait=[("pe", bl)], inc="ve")
                v2 = s.emit("vector", lambda e: e.tensor_mul(T2[:], OG[0:64, :], psBG[:]),
                            wait=[("pe", bg)], inc="ve")
                last["y"] = s.emit("vector", lambda e, qc=qc, base=base, jl=jl: e.tensor_add(
                    Y1[qc][base:base + 64, jl * 128:(jl + 1) * 128], T1[:], T2[:]),
                    wait=[("ve", v2), ("pe", y_war)], inc="ve")
        y_ve = s.n("ve")
        head_ac = s.n("ac")
        s.pfree(psL1, psL2, psG2, psLO, psGO, psBL, psBG)

        if stages <= 4:
            return
        g_wo = load_w(3072, 4096, s.n("pe"))
        psA2 = s.palloc(f"l1psA2_{th}", [128, 512])
        psB2 = s.palloc(f"l1psB2_{th}", [128, 512])
        bo0 = BIAS_COLS["b_out"][0]

        def proj_write(m, n, ps, pe_cnt, t0=t0):
            c2 = s.emit("vector", lambda e, m=m, n=n, ps=ps: e.scalar_tensor_tensor(
                R[m][:, t0 + n * 512:t0 + (n + 1) * 512], ps[:],
                C["bias"][:, bo0 + m:bo0 + m + 1], R[m][:, t0 + n * 512:t0 + (n + 1) * 512],
                ALU.add, ALU.add),
                wait=[("pe", pe_cnt)], inc="ve")
            return ("ve", c2)

        emit_linear_fm(s, [psA2, psB2],
                       lambda k, m: WCUR[:, k * 1024 + m * 128:k * 1024 + (m + 1) * 128],
                       Y1, proj_write, M_tiles=KCH, K_tiles=KCH, n_chunks=2,
                       wait_first=[("ve", y_ve), ("ac", head_ac), g_wo])
        proj_ve = s.n("ve")
        y_war = s.n("pe")
        s.pfree(psA2, psB2)
    s.sfree(*Y1, *QT, *KT, *VS, KGT, VGH, NRM, NEGBL, NEGBG, MASKL, MASKR,
            *XBE, XHL, XHR, XHG, WCUR, SQB,
            PL1, PL2, PG, OL, OG, RZ2, T1, T2)



def emit_l1_ffn(s, C, R, ins):
    nc = s.nc
    wq2s = nc.dram_tensor("wq2scr", [4096, 1024], BF16, kind="Internal").ap()
    WQ1 = [s.salloc(f"wq1_{k}", [128, 4096], BF16) for k in range(KCH)]
    STAT = s.salloc("f4stat", [128, 8], F32)
    ROWS = s.salloc("f4rows", [1, 8], F32)
    WS = [s.salloc(f"f4ws{i}", [128, 4096], F32, side="right") for i in range(2)]
    QA2 = s.salloc("f4qa2", [128, 4096], BF16, side="right")
    QB2 = s.salloc("f4qb2", [128, 4096], BF16, side="right")
    QC2 = s.salloc("f4qc2", [128, 4096], BF16, side="right")
    QD2 = s.salloc("f4qd2", [128, 4096], BF16, side="right")
    QE2 = s.salloc("f4qe2", [128, 4096], BF16, side="right")
    psS1 = s.palloc("f4psS", [1, 8])
    psT = s.palloc("f4psT", [128, 8])

    def quant_pass(wsrc, ntiles, tile_cols, out_write, tag):
        acc_ac = 0
        red_ve = s.n("ve")
        for t in range(ntiles):
            g = s.dma(WS[t % 2][:, 0:tile_cols],
                      wsrc[t * 128:(t + 1) * 128, :], eng="gpsimd",
                      wait=[("ve", red_ve)])
            red_ve = s.emit("vector", lambda e, t=t, tile_cols=tile_cols: e.tensor_reduce(
                STAT[:, 7:8] if t else STAT[:, 0:1], WS[t % 2][:, 0:tile_cols],
                axis=AX.X, op=ALU.add, apply_absolute_value=True),
                wait=[g, ("ac", acc_ac)], inc="ve")
            if t:
                acc_ac = s.emit("scalar", lambda e: e.activation(
                    STAT[:, 0:1], STAT[:, 7:8], AF.Identity, bias=STAT[:, 0:1]),
                    wait=[("ve", red_ve)], inc="ac")
        pe = s.emit("tensor", lambda e: e.matmul(
            psS1[:, 0:1], C["ones_col_f"][:, 0:1], STAT[:, 0:1], start=True, stop=True),
            wait=[("ve", red_ve), ("ac", acc_ac)], inc="pe")
        nelem = float(ntiles * 128 * tile_cols)
        v = s.emit("vector", lambda e, nelem=nelem: e.tensor_scalar(
            ROWS[:, 0:1], psS1[0:1, 0:1], 0.5 / nelem, 0.5e-5, ALU.mult, ALU.add),
            wait=[("pe", pe)], inc="ve")
        pe = s.emit("tensor", lambda e: e.matmul(
            psT[:, 0:1], C["ones_row_f"][0:1, :], ROWS[0:1, 0:1], start=True, stop=True),
            wait=[("ve", v)], inc="pe")
        thr_ve = s.emit("vector", lambda e: e.tensor_copy(STAT[:, 1:2], psT[:, 0:1]),
                        wait=[("pe", pe)], inc="ve")
        thr_ac = s.emit("scalar", lambda e: e.activation(
            STAT[:, 2:3], psT[:, 0:1], AF.Identity, scale=-1.0),
            wait=[("pe", pe)], inc="ac")
        qdone = 0
        q_ac = 0
        for t in range(ntiles):
            g = s.dma(WS[t % 2][:, 0:tile_cols],
                      wsrc[t * 128:(t + 1) * 128, :], eng="gpsimd",
                      wait=[("ve", qdone), ("ac", q_ac)] if t >= 2
                      else [("ve", thr_ve), ("ac", thr_ac)])
            a1 = s.emit("scalar", lambda e, t=t, tile_cols=tile_cols: e.sign(
                QA2[:, 0:tile_cols], WS[t % 2][:, 0:tile_cols], bias=STAT[:, 2:3]),
                wait=[g, ("ve", max(thr_ve, qdone))], inc="ac")
            a2 = s.emit("scalar", lambda e, t=t, tile_cols=tile_cols: e.sign(
                QB2[:, 0:tile_cols], WS[t % 2][:, 0:tile_cols], bias=STAT[:, 1:2]),
                inc="ac")
            a3 = s.emit("scalar", lambda e, tile_cols=tile_cols: e.activation(
                QC2[:, 0:tile_cols], QA2[:, 0:tile_cols], AF.Relu), inc="ac")
            q_ac = s.emit("scalar", lambda e, tile_cols=tile_cols: e.activation(
                QD2[:, 0:tile_cols], QB2[:, 0:tile_cols], AF.Relu, scale=-1.0),
                inc="ac")
            qdone = out_write(t, tile_cols, q_ac)
        return qdone

    def w1_write(t, tile_cols, a4):
        return s.emit("vector", lambda e, t=t, tile_cols=tile_cols: e.tensor_tensor(
            WQ1[t][:, 0:tile_cols], QC2[:, 0:tile_cols], QD2[:, 0:tile_cols],
            ALU.subtract), wait=[("ac", a4)], inc="ve")

    quant_pass(ins["w1T"], KCH, 4096, w1_write, "w1")

    def w2_write(t, tile_cols, a4):
        v3 = s.emit("vector", lambda e, t=t, tile_cols=tile_cols: e.tensor_tensor(
            QE2[:, 0:tile_cols], QC2[:, 0:tile_cols], QD2[:, 0:tile_cols],
            ALU.subtract), wait=[("ac", a4)], inc="ve")
        g = s.dma(wq2s[t * 128:(t + 1) * 128, :], QE2[:, 0:tile_cols], wait=[("ve", v3)],
                  eng="gpsimd")
        c = s.emit("vector", lambda e: e.nop(), wait=[g], inc="ve")
        return c

    w2q_done = quant_pass(ins["w2T"], 32, 1024, w2_write, "w2")
    s.sfree(WS[0], WS[1], QA2, QB2, QC2, QD2, QE2)
    s.pfree(psS1, psT)
    X5B = [s.salloc(f"f4xb{k}", [128, TOK], BF16) for k in range(KCH)]
    xb_ve = 0
    for k in range(KCH):
        xb_ve = s.emit("vector", lambda e, k=k: e.tensor_copy(X5B[k][:], R[k][:]), inc="ve")

    W2S = [s.salloc(f"f4w2s{i}", [128, 1024], BF16, side="right") for i in range(2)]
    HBN = [s.salloc(f"f4hb{f}", [128, 512], BF16) for f in range(32)]
    b10 = BIAS_COLS["b1"][0]
    b20 = BIAS_COLS["b2"][0]
    relu_ac = 0
    out_ve = 0
    w2_pe = 0
    hb_acs = {}
    for n in range(NQ):
        psH = [s.palloc(f"f4psH{n}_{i}", [128, 512]) for i in range(2)]
        for f in range(32):
            pb = f % 2
            pe = None
            for k in range(KCH):
                pe = s.emit("tensor", lambda e, k=k, f=f, n=n, pb=pb, psH=psH: e.matmul(
                    psH[pb][:], WQ1[k][:, f * 128:(f + 1) * 128],
                    X5B[k][:, n * 512:(n + 1) * 512], start=(k == 0), stop=(k == KCH - 1)),
                    wait=[("ac", hb_acs.get(pb, 0)), ("ve", max(xb_ve, out_ve)),
                          ("pe", w2_pe)] if k == 0 else (),
                    inc="pe" if k == KCH - 1 else None)
            hb_acs[pb] = relu_ac = s.emit("scalar", lambda e, f=f, pb=pb, psH=psH: e.activation(
                HBN[f][:], psH[pb][:], AF.Relu,
                bias=C["bias"][:, b10 + f:b10 + f + 1]),
                wait=[("pe", pe)], inc="ac")
        hb_ac = s.n("ac")
        s.pfree(psH[0], psH[1])
        psO = [s.palloc(f"f4psO{n}_{m}", [128, 512]) for m in range(KCH)]
        d_pe = None
        for f in range(32):
            g = s.dma(W2S[f % 2][:], wq2s[f * 128:(f + 1) * 128, :], eng="gpsimd",
                      wait=[("pe", w2_pe), ("ve", w2q_done)])
            for m in range(KCH):
                d_pe = s.emit("tensor", lambda e, f=f, m=m, psO=psO: e.matmul(
                    psO[m][:], W2S[f % 2][:, m * 128:(m + 1) * 128], HBN[f][:],
                    start=(f == 0), stop=(f == 31)),
                    wait=[g, ("ac", hb_ac), ("ve", out_ve)] if m == 0 else (),
                    inc="pe" if m == KCH - 1 else None)
            w2_pe = s.n("pe")
        for m in range(KCH):
            out_ve = s.emit("vector", lambda e, m=m, n=n, psO=psO: e.scalar_tensor_tensor(
                R[m][:, n * 512:(n + 1) * 512], psO[m][:],
                C["bias"][:, b20 + m:b20 + m + 1], R[m][:, n * 512:(n + 1) * 512],
                ALU.add, ALU.add),
                wait=[("pe", d_pe)], inc="ve")
        s.pfree(*psO)
    s.barrier()
    if DEBUG_TAPS:
        for nm, src, shp, dt in (("dbg_stat", STAT[:], [128, 8], F32),
                                 ("dbg_wq10", WQ1[0][:], [128, 4096], BF16),
                                 ("dbg_hbn0", HBN[0][:], [128, 512], BF16)):
            oo = nc.dram_tensor(nm, shp, dt, kind="ExternalOutput").ap()
            s.dma_out(oo, src)
        oo = nc.dram_tensor("dbg_wq2", [4096, 1024], BF16, kind="ExternalOutput").ap()
        s.dma_out(oo, wq2s)
        for i2 in range(2):
            oo = nc.dram_tensor(f"dbg_w2s{i2}", [128, 1024], BF16,
                                kind="ExternalOutput").ap()
            s.dma_out(oo, W2S[i2][:])
        oo = nc.dram_tensor("dbg_r0", [128, TOK], F32, kind="ExternalOutput").ap()
        s.dma_out(oo, R[0][:])
        oo = nc.dram_tensor("dbg_hbn5", [128, 512], BF16, kind="ExternalOutput").ap()
        s.dma_out(oo, HBN[5][:])
        s.barrier()
    s.sfree(*X5B, *WQ1, STAT, ROWS, *W2S, *HBN)



def common_prolog(nc, s):
    C = {}
    C["bias"] = nc.alloc_sbuf_tensor("biasv", [128, NBIAS], F32)
    R = [nc.alloc_sbuf_tensor(f"resid{k}", [128, TOK], F32) for k in range(KCH)]
    emit_const_setup(s, C)
    biasp = nc.dram_tensor("biasp", [128, NBIAS], F32, kind="ExternalInput").ap()
    s.dma(C["bias"][:], biasp)
    return C, R


def load_stream(s, R, xin):
    for k in range(KCH):
        s.dma(R[k][:], xin[k * 128:(k + 1) * 128, :])
    s.barrier()


def store_stream(s, R, out):
    s.barrier()
    st = None
    for k in range(KCH):
        st = s.dma_out(out[k * 128:(k + 1) * 128, :], R[k][:])
    s.emit("sync", lambda e: e.nop(), wait=[("st", st)])


def build_l0_attn_launch():
    nc = bass.Bass("TRN2", target_bir_lowering=False)
    ins = {
        "xbT": nc.dram_tensor("xbT", [D, BATCH_T], F32, kind="ExternalInput").ap(),
        "wqdT": nc.dram_tensor("wqdT", [1024, 512], F32, kind="ExternalInput").ap(),
        "wqkT": nc.dram_tensor("wqkT", [512, 1024], F32, kind="ExternalInput").ap(),
        "wkvdT": nc.dram_tensor("wkvdT", [1024, 64], F32, kind="ExternalInput").ap(),
        "wvuT": nc.dram_tensor("wvuT", [64, 1024], F32, kind="ExternalInput").ap(),
        "wprojT": nc.dram_tensor("wprojT", [1024, 1024], F32, kind="ExternalInput").ap(),
    }
    xin = nc.dram_tensor("xinT", [D, TOK], F32, kind="ExternalInput").ap()
    out = nc.dram_tensor("xoutT", [D, TOK], F32, kind="ExternalOutput").ap()
    s = Sched(nc)
    C, R = common_prolog(nc, s)
    load_stream(s, R, xin)
    emit_l0_attn(s, C, R, ins)
    s.barrier()
    if DEBUG_TAPS:
        for k in range(KCH):
            o = nc.dram_tensor(f"dbg_r{k}", [128, TOK], F32, kind="ExternalOutput").ap()
            s.dma_out(o, R[k][:])
        s.barrier()
    emit_ln(s, C, R)
    store_stream(s, R, out)
    s.play()
    return nc


def build_moe_launch():
    nc = bass.Bass("TRN2", target_bir_lowering=False)
    ins = {
        "routerT": nc.dram_tensor("routerT", [1024, 8], F32, kind="ExternalInput").ap(),
        "gate": nc.dram_tensor("gatew", [8, 1024, 2048], F32, kind="ExternalInput").ap(),
        "up": nc.dram_tensor("upw", [8, 1024, 2048], F32, kind="ExternalInput").ap(),
        "down": nc.dram_tensor("downw", [8, 2048, 1024], F32, kind="ExternalInput").ap(),
        "shared_gate": nc.dram_tensor("sgw", [1024, 2048], F32, kind="ExternalInput").ap(),
        "shared_up": nc.dram_tensor("suw", [1024, 2048], F32, kind="ExternalInput").ap(),
        "shared_down": nc.dram_tensor("sdw", [2048, 1024], F32, kind="ExternalInput").ap(),
    }
    xin = nc.dram_tensor("xinT", [D, TOK], F32, kind="ExternalInput").ap()
    out = nc.dram_tensor("xoutT", [D, TOK], F32, kind="ExternalOutput").ap()
    pack = nc.dram_tensor("packT", [D, 160], F32, kind="ExternalOutput").ap()
    s = Sched(nc)
    C, R = common_prolog(nc, s)
    load_stream(s, R, xin)
    emit_moe(s, C, R, ins)
    emit_ln(s, C, R)
    s.barrier()
    emit_pack(s, R, pack)
    store_stream(s, R, out)
    s.play()
    return nc


def run_moe(x2T, inputs, ip):
    nc = build_moe_launch()
    in_maps = []
    for c in range(NCORES):
        b, i = c // 4, c % 4
        in_maps.append({
            "xinT": np.ascontiguousarray(x2T[b][:, i * TOK:(i + 1) * TOK]),
            "routerT": ip["routerT"],
            "gatew": np.asarray(inputs["L0_ffn_expert_gate"]),
            "upw": np.asarray(inputs["L0_ffn_expert_up"]),
            "downw": np.asarray(inputs["L0_ffn_expert_down"]),
            "sgw": np.asarray(inputs["L0_ffn_shared_gate"]),
            "suw": np.asarray(inputs["L0_ffn_shared_up"]),
            "sdw": np.asarray(inputs["L0_ffn_shared_down"]),
            "biasp": ip["biasp"],
        })
    res = run_bass_kernel_spmd(nc, in_maps, core_ids=list(range(NCORES)))
    out = np.zeros((2, D, BATCH_T), np.float32)
    packs = np.zeros((NCORES, D, 160), np.float32)
    for c in range(NCORES):
        b, i = c // 4, c % 4
        out[b][:, i * TOK:(i + 1) * TOK] = res.results[c]["xoutT"]
        packs[c] = res.results[c]["packT"]
    return out, packs


def build_l1_attn_launch(stages=99):
    nc = bass.Bass("TRN2", target_bir_lowering=False)
    ins = {
        "wqkv": nc.dram_tensor("wqkv", [1024, 3072], F32, kind="ExternalInput").ap(),
        "wout": nc.dram_tensor("wout", [1024, 1024], F32, kind="ExternalInput").ap(),
        "haloL": nc.dram_tensor("haloL", [1024, 64], F32, kind="ExternalInput").ap(),
        "haloR": nc.dram_tensor("haloR", [1024, 64], F32, kind="ExternalInput").ap(),
        "haloG": nc.dram_tensor("haloG", [1024, 128], F32, kind="ExternalInput").ap(),
        "maskL": nc.dram_tensor("maskL", [128, 1], F32, kind="ExternalInput").ap(),
        "maskR": nc.dram_tensor("maskR", [64, 1], F32, kind="ExternalInput").ap(),
    }
    xin = nc.dram_tensor("xinT", [D, TOK], F32, kind="ExternalInput").ap()
    out = nc.dram_tensor("xoutT", [D, TOK], F32, kind="ExternalOutput").ap()
    s = Sched(nc)
    C, R = common_prolog(nc, s)
    load_stream(s, R, xin)
    emit_l1_attn(s, C, R, ins, stages=stages)
    s.barrier()
    if stages > 50:
        emit_ln(s, C, R)
    store_stream(s, R, out)
    s.play()
    return nc


def l1_halo_maps(xT, inputs, ip):
    zeros64 = np.zeros((D, 64), np.float32)
    maps = []
    for c in range(NCORES):
        b, i = c // 4, c % 4
        mL = np.ones((128, 1), np.float32)
        mR = np.ones((64, 1), np.float32)
        if i == 0:
            mL[0:32] = 0.0
        if i == 3:
            mR[32:64] = 0.0
        maps.append({
            "xinT": np.ascontiguousarray(xT[b][:, i * TOK:(i + 1) * TOK]),
            "haloL": np.ascontiguousarray(xT[b][:, i * TOK - 64:i * TOK]) if i > 0 else zeros64,
            "haloR": np.ascontiguousarray(xT[b][:, (i + 1) * TOK:(i + 1) * TOK + 64]) if i < 3 else zeros64,
            "haloG": np.ascontiguousarray(xT[b][:, ::64]),
            "maskL": mL, "maskR": mR,
            "wqkv": np.asarray(inputs["L1_attn_w_qkv"]),
            "wout": np.asarray(inputs["L1_attn_w_out"]),
            "biasp": ip["biasp"],
        })
    return maps


def run_l1_attn(xT, inputs, ip):
    nc = build_l1_attn_launch()
    in_maps = l1_halo_maps(xT, inputs, ip)
    res = run_bass_kernel_spmd(nc, in_maps, core_ids=list(range(NCORES)))
    out = np.zeros((2, D, BATCH_T), np.float32)
    for c in range(NCORES):
        b, i = c // 4, c % 4
        out[b][:, i * TOK:(i + 1) * TOK] = res.results[c]["xoutT"]
    return out


def build_l1_ffn_launch():
    nc = bass.Bass("TRN2", target_bir_lowering=False)
    ins = {
        "w1T": nc.dram_tensor("w1T", [1024, 4096], F32, kind="ExternalInput").ap(),
        "w2T": nc.dram_tensor("w2T", [4096, 1024], F32, kind="ExternalInput").ap(),
    }
    xin = nc.dram_tensor("xinT", [D, TOK], F32, kind="ExternalInput").ap()
    out = nc.dram_tensor("xoutT", [D, TOK], F32, kind="ExternalOutput").ap()
    s = Sched(nc)
    C, R = common_prolog(nc, s)
    load_stream(s, R, xin)
    emit_l1_ffn(s, C, R, ins)
    emit_ln(s, C, R)
    store_stream(s, R, out)
    s.play()
    return nc


def run_l1_ffn(xT, inputs, ip):
    nc = build_l1_ffn_launch()
    in_maps = []
    for c in range(NCORES):
        b, i = c // 4, c % 4
        in_maps.append({
            "xinT": np.ascontiguousarray(xT[b][:, i * TOK:(i + 1) * TOK]),
            "w1T": ip["w1T"], "w2T": ip["w2T"], "biasp": ip["biasp"],
        })
    res = run_bass_kernel_spmd(nc, in_maps, core_ids=list(range(NCORES)))
    out = np.zeros((2, D, BATCH_T), np.float32)
    for c in range(NCORES):
        b, i = c // 4, c % 4
        out[b][:, i * TOK:(i + 1) * TOK] = res.results[c]["xoutT"]
    return out


def host_prep(inputs):
    ip = {}
    ip["xT"] = np.ascontiguousarray(np.transpose(np.asarray(inputs["L0_attn_x"]), (0, 2, 1)))
    for nm, key in [("wqdT", "L0_attn_wq_d"), ("wqkT", "L0_attn_w_qk"),
                    ("wkvdT", "L0_attn_wkv_d"), ("wvuT", "L0_attn_wv_u"),
                    ("wprojT", "L0_attn_w_proj"), ("routerT", "L0_ffn_router_w"),
                    ("w1T", "L1_ffn_W1"), ("w2T", "L1_ffn_W2")]:
        ip[nm] = np.ascontiguousarray(np.asarray(inputs[key]).T)
    cols = np.zeros((128, NBIAS), np.float32)

    def put(name, v):
        c0, nn = BIAS_COLS[name]
        v = np.asarray(v, np.float32).reshape(-1)
        if name == "bkv_d":
            cols[0:64, c0] = v
            cols[64:128, c0] = v
        else:
            cols[:, c0:c0 + nn] = v.reshape(nn, 128).T
    put("bq_d", inputs["L0_attn_bq_d"]); put("b_qk", inputs["L0_attn_b_qk"])
    put("bkv_d", inputs["L0_attn_bkv_d"]); put("bv_u", inputs["L0_attn_bv_u"])
    put("b_proj", inputs["L0_attn_b_proj"]); put("b_out", inputs["L1_attn_b_out"])
    put("b1", inputs["L1_ffn_b1"]); put("b2", inputs["L1_ffn_b2"])
    ip["biasp"] = cols
    return ip


def run_l0_attn(inputs, ip=None):
    ip = ip or host_prep(inputs)
    nc = build_l0_attn_launch()
    in_maps = []
    for c in range(NCORES):
        b, i = c // 4, c % 4
        in_maps.append({
            "xbT": ip["xT"][b],
            "xinT": np.ascontiguousarray(ip["xT"][b][:, i * TOK:(i + 1) * TOK]),
            "wqdT": ip["wqdT"], "wqkT": ip["wqkT"], "wkvdT": ip["wkvdT"],
            "wvuT": ip["wvuT"], "wprojT": ip["wprojT"], "biasp": ip["biasp"],
        })
    res = run_bass_kernel_spmd(nc, in_maps, core_ids=list(range(NCORES)))
    out = np.zeros((2, D, BATCH_T), np.float32)
    for c in range(NCORES):
        b, i = c // 4, c % 4
        out[b][:, i * TOK:(i + 1) * TOK] = res.results[c]["xoutT"]
    return out


def kernel(**inputs):
    ip = host_prep(inputs)
    xT = run_l0_attn(inputs, ip)
    xT, packs = run_moe(xT, inputs, ip)
    xT = run_l1_attn(xT, inputs, ip)
    xT = run_l1_ffn(xT, inputs, ip)
    return np.ascontiguousarray(np.transpose(xT, (0, 2, 1)))
